# version 01
# baseline (speedup 1.0000x reference)
import numpy as np
import ml_dtypes
import concourse.bass as bass
import concourse.mybir as mybir
from concourse.bass_utils import run_bass_kernel_spmd

F32 = mybir.dt.float32
BF16 = mybir.dt.bfloat16
AF = mybir.ActivationFunctionType
ALU = mybir.AluOpType
AX = mybir.AxisListType

NCORE = 8
D = 4096
TC = 1024
S = 8192
TG = 512
NTG = 2
DCH = 32
HD = 128
NKV = 8
NQH_A = 4
KV = 1024
NCH = 64
QCH = 16
ER = 128
EPS = 1e-5
SOFT = float(1.0 / np.sqrt(128.0))
RMEAN = 1.0 / 7580.0

ENGS = ("tensor", "scalar", "vector", "gpsimd", "sync")


class Sched:
    def __init__(self, nc):
        self.nc = nc
        self.q = {e: [] for e in ENGS}
        self.h = {}
        self.c = {}

    def sem(self, name):
        self.h[name] = self.nc.alloc_semaphore(name)
        self.c[name] = 0

    def wait(self, eng, sem, val=None):
        v = self.c[sem] if val is None else val
        if v <= 0:
            return
        h = self.h[sem]
        self.q[eng].append(lambda E, h=h, v=v: E.wait_ge(h, v))

    def op(self, eng, fn, inc=None, inc_val=1):
        if inc is None:
            self.q[eng].append(fn)
            return None
        h = self.h[inc]
        self.c[inc] += inc_val
        self.q[eng].append(lambda E, fn=fn, h=h, iv=inc_val: fn(E).then_inc(h, iv))
        return self.c[inc]

    def barrier(self):
        h = self.h["s_bar"]
        self.c["s_bar"] += len(ENGS)
        tgt = self.c["s_bar"]
        for e in ENGS:
            self.q[e].append(lambda E, h=h: E.sem_inc(h, 1))
        for e in ENGS:
            self.q[e].append(lambda E, h=h, v=tgt: E.wait_ge(h, v))


class KB:
    def __init__(self, n_sub=4, dbg=False):
        self.n_sub = n_sub
        self.dbg = dbg
        nc = bass.Bass("TRN2", target_bir_lowering=False)
        self.nc = nc

        def ein(name, shape, dt=F32):
            return nc.dram_tensor(name, shape, dt, kind="ExternalInput")
        self.x_t = ein("x_t", [D, TC])
        self.wq_a = ein("wq_a", [D, D])
        self.wk_a = ein("wk_a", [D, KV])
        self.wv_a = ein("wv_a", [D, KV])
        self.wo_a = ein("wo_a", [D, D])
        self.wq_b = ein("wq_b", [D, D])
        self.wk_b = ein("wk_b", [D, KV])
        self.wv_b = ein("wv_b", [D, KV])
        self.wo_b = ein("wo_b", [KV, D])
        self.wff = ein("wff", [D, D])
        self.bff = ein("bff", [128, DCH])
        self.a_all = ein("a_all", [D, ER])
        self.b_all = ein("b_all", [ER, D])
        self.rw_t = ein("rw_t", [D, 8])
        self.rbv_i = ein("rbv_i", [1, 8])
        self.mask_i = ein("mask_i", [1, TC], BF16)
        self.ind_i = ein("ind_i", [8, ER])
        self.y_t = nc.dram_tensor("y_t", [D, TC], F32, kind="ExternalOutput")
        if dbg:
            self.q_dbg = nc.dram_tensor("q_dbg", [D, TC], BF16,
                                        kind="ExternalOutput")
            self.k_dbg = nc.dram_tensor("k_dbg", [NCORE * KV, TC], BF16,
                                        kind="ExternalOutput")
            self.ao_dbg = nc.dram_tensor("ao_dbg", [D, TC], BF16,
                                         kind="ExternalOutput")
            self.v_dbg = nc.dram_tensor("v_dbg", [NCORE * 128, 8 * KV], BF16,
                                        kind="ExternalOutput")
            self.r_dbg = nc.dram_tensor("r_dbg", [128, 64], F32,
                                        kind="ExternalOutput")

        dram = nc.dram_tensor
        self.qa_d = dram("qa_d", [D, TC], BF16)
        self.qb_d = dram("qb_d", [KV, TC], BF16)
        self.kin_d = dram("kin_d", [KV, TC], BF16)
        self.vin_d = dram("vin_d", [128, 8 * KV], BF16)
        self.kag_d = dram("kag_d", [NCORE * KV, TC], BF16, addr_space="Shared")
        self.vag_d = dram("vag_d", [NCORE * 128, 8 * KV], BF16, addr_space="Shared")
        self.ao_d = dram("ao_d", [D, TC], BF16)
        self.aob_d = dram("aob_d", [KV, TC], BF16)
        self.weff_d = dram("weff_d", [D, KV], BF16)
        self.rtr_in = dram("rtr_in", [128, DCH], F32)
        self.rtr_out = dram("rtr_out", [128, DCH], F32, addr_space="Shared")
        self.km_in = dram("km_in", [1, 4], F32)
        self.km_out = dram("km_out", [1, 4], F32, addr_space="Shared")
        self.rt8_d = dram("rt8_d", [8, 1], F32)

        sb = nc.alloc_sbuf_tensor
        self.xsb = sb("xsb", [128, DCH * TC], F32)
        self.wbuf = sb("wbuf", [128, 2 * 4096], BF16)
        base, _ = nc.bump_sbuf(DCH * TG * 2)
        at = nc.alloc_sbuf_tensor_at
        self.lnx = at("lnx", [128, DCH * TG], BF16, offset=base)
        self.kh = at("kh", [128, 2 * QCH * 128], BF16, offset=base)
        self.vh = at("vh", [128, 2 * QCH * 128], BF16, offset=base + 8192)
        self.qtl = at("qtl", [128, NQH_A * TG], BF16, offset=base + 16384)
        self.p_sb = at("p_sb", [128, 4 * TG], BF16, offset=base + 20480)
        self.sum_acc = at("sum_acc", [128, 4 * TG], F32, offset=base + 24576)
        self.wstage = self.wbuf
        self.weff_sb = at("weff_sb", [128, KV], BF16,
                          offset=self._addr(self.wbuf) + 8192)
        self.stg = sb("stg", [128, 2 * TG], BF16)
        self.cast = sb("cast", [128, 2 * TG], BF16)
        self.stg_ao = self.cast
        self.sqb = sb("sqb", [128, 2 * TG], BF16)
        self.rmul = sb("rmul", [128, TG], F32)
        self.inv_sb = self.rmul
        self.ba_oc = sb("ba_oc", [128, 2 * 128], BF16)
        self.mask_b = sb("mask_b", [1, TC], BF16)
        self.tprime = sb("tprime", [128, TG], BF16)
        self.mean_v = sb("mean_v", [1, TG], F32)
        self.rstd_v = sb("rstd_v", [1, TG], F32)
        self.negmr_v = sb("negmr_v", [1, TG], F32)
        self.sv_rstd = sb("sv_rstd", [1, TG], F32)
        self.sv_negmr = sb("sv_negmr", [1, TG], F32)
        self.sm_a = sb("sm_a", [1, TG], F32)
        self.sm_b = sb("sm_b", [1, TG], F32)
        self.w1_v = sb("w1_v", [1, TG], F32)
        self.inv_vec = self.sm_a
        self.rw_sb = sb("rw_sb", [128, DCH * 8], F32)
        self.q_sb = sb("q_sb", [128, DCH], F32)
        self.qp0 = sb("qp0", [128, DCH], F32)
        self.qp1 = sb("qp1", [128, DCH], F32)
        self.ind_sb = sb("ind_sb", [8, ER], F32)
        self.bff_sb = sb("bff_sb", [128, DCH], F32)
        self.rtT = sb("rtT", [8, 1], F32)
        self.rbc = sb("rbc", [128, 1], F32)
        self.negc_b = sb("negc_b", [128, 1], F32)
        self.t2b = sb("t2b", [128, 1], F32)
        self.ones_f1 = sb("ones_f1", [1, 128], F32)
        self.ones_bf = sb("ones_bf", [128, 1], BF16)
        self.ones_fc = sb("ones_fc", [128, 1], F32)
        self.kmx_sb = sb("kmx_sb", [1, 4], F32)
        self.tiny = sb("tiny", [1, 32], F32)
        self.rbv = sb("rbv", [1, 8], F32)
        self.lg = sb("lg", [1, 8], F32)
        self.qm2 = sb("qm2", [1, 1], F32)
        self.km2 = sb("km2", [1, 1], F32)
        self.const_sb = sb("const_sb", [1, 4], F32)
        nc.const_aps.register(F32, EPS, self.const_sb[0:1, 0:1])
        print(f"[kb] sbuf remaining/partition: {nc.sbuf_bytes_remaining} bytes")

        self.P = [nc.alloc_psum_tensor(f"pb{i}", [128, TG], F32) for i in range(8)]

        s = Sched(nc)
        self.s = s
        for name in ("s_bar", "s_w", "s_ld", "s_st", "s_pe", "s_ac", "s_dv",
                     "s_ps", "s_cc_k1", "s_cc_v1", "s_cc_k2", "s_cc_v2",
                     "s_cc_m1", "s_cc_m2", "s_cc_r1", "s_cc_r2"):
            s.sem(name)
        self.presum_done = 0

    def _addr(self, h):
        return self.nc.lookup_mloc(h).addr

    def xs(self, dc, tg):
        return self.xsb[:, dc * TC + tg * TG: dc * TC + tg * TG + TG]

    def lx(self, dc):
        return self.lnx[:, dc * TG: (dc + 1) * TG]

    def wslot(self, par, dc):
        return self.wbuf[:, par * 4096 + dc * 128: par * 4096 + (dc + 1) * 128]

    def ph_init(self):
        s = self.s
        s.op("vector", lambda E: E.memset(self.ones_f1[:, :], 1.0))
        s.op("vector", lambda E: E.memset(self.ones_bf[:, :], 1.0))
        s.op("vector", lambda E: E.memset(self.ones_fc[:, :], 1.0))
        s.op("vector", lambda E: E.memset(self.tiny[:, :], 0.0))
        s.op("vector", lambda E: E.memset(self.const_sb[0:1, 0:1], EPS))
        s.op("vector", lambda E: E.memset(self.lg[:, :], 0.0), inc="s_dv")
        s.op("sync", lambda E: E.dma_start(
            out=self.xsb[:, :].rearrange("p (dc t) -> p dc t", dc=DCH),
            in_=self.x_t.ap().rearrange("(dc p) t -> p dc t", p=128)),
            inc="s_ld", inc_val=16)
        s.op("sync", lambda E: E.dma_start(out=self.mask_b[:, :],
                                           in_=self.mask_i[:, :]),
             inc="s_ld", inc_val=16)
        s.op("sync", lambda E: E.dma_start(
            out=self.rw_sb[:, :].rearrange("p (dc e) -> p dc e", dc=DCH),
            in_=self.rw_t.ap().rearrange("(dc p) e -> p dc e", p=128)),
            inc="s_ld", inc_val=16)
        s.op("sync", lambda E: E.dma_start(out=self.ind_sb[:, :],
                                           in_=self.ind_i[:, :]),
             inc="s_ld", inc_val=16)
        s.op("sync", lambda E: E.dma_start(out=self.bff_sb[:, :],
                                           in_=self.bff.ap()),
             inc="s_ld", inc_val=16)
        s.op("sync", lambda E: E.dma_start(out=self.rbv[:, :],
                                           in_=self.rbv_i.ap()),
             inc="s_ld", inc_val=16)
        self.x_ld = s.c["s_ld"]

    def ph_ln_stats(self, tg):
        s = self.s
        P = self.P
        s.wait("vector", "s_ld", self.x_ld)
        s.wait("vector", "s_pe")
        pe_base = s.c["s_pe"]
        dv_marks = []
        for dc in range(DCH):
            par = dc % 2
            if dc >= 2:
                s.wait("vector", "s_pe", pe_base + dc - 1)
            s.op("vector", lambda E, dc=dc, par=par, tg=tg: E.tensor_copy(
                self.cast[:, par * TG:(par + 1) * TG], self.xs(dc, tg)))
            s.op("vector", lambda E, par=par: E.tensor_mul(
                self.sqb[:, par * TG:(par + 1) * TG],
                self.cast[:, par * TG:(par + 1) * TG],
                self.cast[:, par * TG:(par + 1) * TG]), inc="s_dv")
            dv_marks.append(s.c["s_dv"])
        for dc in range(DCH):
            par = dc % 2
            s.wait("tensor", "s_dv", dv_marks[dc])
            s.op("tensor", lambda E, par=par, dc=dc: E.matmul(
                P[4][0:1, :], self.ones_bf[:, :],
                self.cast[:, par * TG:(par + 1) * TG],
                start=(dc == 0), stop=(dc == DCH - 1)))
            s.op("tensor", lambda E, par=par, dc=dc: E.matmul(
                P[5][0:1, :], self.ones_bf[:, :],
                self.sqb[:, par * TG:(par + 1) * TG],
                start=(dc == 0), stop=(dc == DCH - 1)), inc="s_pe")
        s.wait("vector", "s_pe")
        s.op("vector", lambda E: E.tensor_scalar_mul(
            self.mean_v[:, :], P[4][0:1, :], 1.0 / D))
        s.op("vector", lambda E: E.tensor_scalar_mul(
            self.sm_a[:, :], P[5][0:1, :], 1.0 / D))
        s.op("vector", lambda E: E.tensor_mul(
            self.sm_b[:, :], self.mean_v[:, :], self.mean_v[:, :]))
        s.op("vector", lambda E: E.tensor_sub(
            self.sm_a[:, :], self.sm_a[:, :], self.sm_b[:, :]), inc="s_dv")
        s.wait("scalar", "s_dv")
        s.op("scalar", lambda E: E.activation(
            self.sm_b[:, :], self.sm_a[:, :], AF.Ln, bias=EPS))
        s.op("scalar", lambda E: E.activation(
            self.rstd_v[:, :], self.sm_b[:, :], AF.Exp, scale=-0.5), inc="s_ac")
        s.wait("vector", "s_ac")
        s.op("vector", lambda E: E.scalar_tensor_tensor(
            self.negmr_v[:, :], self.mean_v[:, :], -1.0, self.rstd_v[:, :],
            op0=ALU.mult, op1=ALU.mult), inc="s_dv")

    def ph_ln_apply(self, tg, rstd=None, negmr=None):
        s = self.s
        P = self.P
        rstd = rstd if rstd is not None else self.rstd_v
        negmr = negmr if negmr is not None else self.negmr_v
        s.wait("tensor", "s_dv")
        s.op("tensor", lambda E, rstd=rstd: E.matmul(
            P[6][:, :], self.ones_f1[:, :], rstd[:, :], start=True, stop=True))
        s.op("tensor", lambda E, negmr=negmr: E.matmul(
            P[7][:, :], self.ones_f1[:, :], negmr[:, :], start=True, stop=True),
            inc="s_pe")
        s.wait("vector", "s_pe")
        if getattr(self, "lnx_user_pe", 0):
            s.wait("vector", "s_pe", self.lnx_user_pe)
        for dc in range(DCH):
            s.op("vector", lambda E, dc=dc, tg=tg: E.tensor_tensor(
                self.rmul[:, :], self.xs(dc, tg), P[6][:, :], ALU.mult))
            s.op("vector", lambda E, dc=dc: E.tensor_tensor(
                self.lx(dc), self.rmul[:, :], P[7][:, :], ALU.add))
        s.op("vector", lambda E: E.engine_nop(), inc="s_dv")
        self.lnx_dv = s.c["s_dv"]

    def weight_col_load(self, w, oc, par, nrows=D):
        s = self.s
        ndc = nrows // 128
        half = max(1, ndc // 2)
        for h0 in range(0, ndc, half):
            cnt = min(half, ndc - h0)
            def f(E, w=w, oc=oc, par=par, h0=h0, cnt=cnt):
                src = w.ap().rearrange("(dc p) o -> p dc o", p=128)[
                    :, h0:h0 + cnt, oc * 128:(oc + 1) * 128]
                dst = self.wbuf[:, par * 4096 + h0 * 128:
                                par * 4096 + (h0 + cnt) * 128
                                ].rearrange("p (dc o) -> p dc o", o=128)
                return E.dma_start(out=dst, in_=src)
            s.op("gpsimd", f, inc="s_w", inc_val=16)
        return s.c["s_w"]

    def ph_proj(self, tg, w, ncols, dest, norm_acc=None, nrows=D):
        s = self.s
        P = self.P
        noc = ncols // 128
        ndc = nrows // 128
        wbuf_guard = s.c["s_pe"]
        s.wait("tensor", "s_ac")
        s.wait("tensor", "s_dv")
        pe_marks = []
        ac_marks = []
        st_marks = []
        nrm = {}
        for oc in range(noc):
            s.wait("gpsimd", "s_pe",
                   pe_marks[oc - 2] if oc >= 2 else wbuf_guard)
            wm = self.weight_col_load(w, oc, oc % 2, nrows)
            s.wait("tensor", "s_w", wm)
            if oc == 0:
                s.wait("tensor", "s_dv", self.lnx_dv)
            if oc >= 4:
                s.wait("tensor", "s_ac", ac_marks[oc - 4])
            bank = oc % 4
            for dc in range(ndc):
                s.op("tensor", lambda E, oc=oc, dc=dc, bank=bank: E.matmul(
                    P[bank][:, :], self.wslot(oc % 2, dc), self.lx(dc),
                    start=(dc == 0), stop=(dc == ndc - 1)),
                    inc="s_pe" if dc == ndc - 1 else None)
            pe_marks.append(s.c["s_pe"])
            s.wait("scalar", "s_pe", pe_marks[oc])
            if oc >= 2:
                s.wait("scalar", "s_st", st_marks[oc - 2])
                if norm_acc:
                    s.wait("scalar", "s_dv", nrm[oc - 2])
            sp = oc % 2
            s.op("scalar", lambda E, bank=bank, sp=sp: E.activation(
                self.stg[:, sp * TG:(sp + 1) * TG], P[bank][:, :], AF.Copy),
                inc="s_ac")
            ac_marks.append(s.c["s_ac"])
            if norm_acc:
                tgt = self.qm2 if norm_acc == "q" else self.km2
                s.wait("vector", "s_ac", ac_marks[oc])
                s.op("vector", lambda E, sp=sp: E.tensor_mul(
                    self.sqb[:, sp * TG:(sp + 1) * TG],
                    self.stg[:, sp * TG:(sp + 1) * TG],
                    self.stg[:, sp * TG:(sp + 1) * TG]), inc="s_dv")
                s.wait("tensor", "s_dv", s.c["s_dv"])
                s.op("tensor", lambda E, sp=sp: E.matmul(
                    P[7][0:1, :], self.ones_bf[:, :],
                    self.sqb[:, sp * TG:(sp + 1) * TG], start=True, stop=True),
                    inc="s_pe")
                s.wait("vector", "s_pe", s.c["s_pe"])
                s.op("vector", lambda E: E.reduce_max(
                    self.tiny[:, 0:1], P[7][0:1, :], AX.X), inc="s_dv")
                s.wait("vector", "s_dv")
                s.op("vector", lambda E, tgt=tgt: E.tensor_tensor(
                    tgt[:, :], self.tiny[:, 0:1], tgt[:, :], ALU.max),
                    inc="s_dv")
                nrm[oc] = s.c["s_dv"]
            s.wait("sync", "s_ac", ac_marks[oc])
            def fst(E, oc=oc, sp=sp, dest=dest, tg=tg):
                return E.dma_start(
                    out=dest[oc * 128:(oc + 1) * 128, tg * TG:(tg + 1) * TG],
                    in_=self.stg[:, sp * TG:(sp + 1) * TG])
            s.op("sync", fst, inc="s_st", inc_val=16)
            st_marks.append(s.c["s_st"])
        self.proj_st = s.c["s_st"]
        self.proj_pe = pe_marks[-1]

    def ph_proj_v(self, tg, w):
        s = self.s
        P = self.P
        wbuf_guard = s.c["s_pe"]
        s.wait("tensor", "s_ac")
        s.wait("tensor", "s_dv")
        pe_marks = {}
        st_list = []
        nslot = 16
        sl = 0
        last_round_pe = None
        for dvh in range(2):
            for dc in range(DCH):
                slot = sl % nslot
                s.wait("gpsimd", "s_pe",
                       pe_marks[sl - nslot] if sl >= nslot else wbuf_guard)
                def fw(E, dc=dc, dvh=dvh, slot=slot, w=w):
                    src = w.ap()[dc * 128:(dc + 1) * 128, dvh * TG:(dvh + 1) * TG]
                    return E.dma_start(
                        out=self.wbuf[:, slot * TG:(slot + 1) * TG], in_=src)
                s.op("gpsimd", fw, inc="s_w", inc_val=16)
                s.wait("tensor", "s_w", s.c["s_w"])
                if dvh == 1 and dc == 0:
                    s.wait("tensor", "s_ac")
                for tc in range(4):
                    s.op("tensor", lambda E, dc=dc, tc=tc, slot=slot: E.matmul(
                        P[tc][:, :],
                        self.lnx[:, dc * TG + tc * 128: dc * TG + (tc + 1) * 128],
                        self.wbuf[:, slot * TG:(slot + 1) * TG],
                        start=(dc == 0), stop=(dc == DCH - 1)),
                        inc="s_pe" if tc == 3 else None)
                pe_marks[sl] = s.c["s_pe"]
                sl += 1
            s.wait("scalar", "s_pe", pe_marks[sl - 1])
            for tc in range(4):
                sp = tc % 2
                if len(st_list) >= 2:
                    s.wait("scalar", "s_st", st_list[-2])
                s.op("scalar", lambda E, tc=tc, sp=sp: E.activation(
                    self.stg[:, sp * TG:(sp + 1) * TG], P[tc][:, :], AF.Copy),
                    inc="s_ac")
                am = s.c["s_ac"]
                s.wait("sync", "s_ac", am)
                def fst(E, tc=tc, sp=sp, dvh=dvh, tg=tg):
                    dst = self.vin_d.ap().rearrange(
                        "p (h kc j) -> p h kc j", h=8, kc=8)[
                        :, 4 * dvh:4 * dvh + 4, tg * 4 + tc, :]
                    src = self.stg[:, sp * TG:(sp + 1) * TG].rearrange(
                        "p (h j) -> p h j", j=128)
                    return E.dma_start(out=dst, in_=src)
                s.op("sync", fst, inc="s_st", inc_val=16)
                st_list.append(s.c["s_st"])
        self.proj_st = s.c["s_st"]
        self.proj_pe = pe_marks[sl - 1]

    def ph_presum_wqb(self):
        s = self.s
        dv_marks = {}
        ps_marks = {}
        wbuf_guard = s.c["s_pe"]
        for dc in range(DCH):
            if dc == 0:
                s.wait("gpsimd", "s_pe", wbuf_guard)
            else:
                s.wait("gpsimd", "s_dv", dv_marks[dc - 1])
            def fl(E, dc=dc):
                return E.dma_start(
                    out=self.wbuf[:, 0:4096],
                    in_=self.wq_b.ap()[dc * 128:(dc + 1) * 128, :])
            s.op("gpsimd", fl, inc="s_ps", inc_val=16)
            lm = s.c["s_ps"]
            s.wait("vector", "s_ps", lm)
            if dc >= 1:
                s.wait("vector", "s_ps", ps_marks[dc - 1])
            w4 = self.wbuf[:, 0:4096].rearrange("p (h g j) -> p h g j", g=4, j=128)
            we = self.weff_sb[:, :].rearrange("p (h j) -> p h j", j=128)
            s.op("vector", lambda E, w4=w4, we=we: E.tensor_tensor(
                we, w4[:, :, 0, :], w4[:, :, 1, :], ALU.add))
            s.op("vector", lambda E, w4=w4, we=we: E.tensor_tensor(
                we, we, w4[:, :, 2, :], ALU.add))
            s.op("vector", lambda E, w4=w4, we=we: E.tensor_tensor(
                we, we, w4[:, :, 3, :], ALU.add), inc="s_dv")
            dv_marks[dc] = s.c["s_dv"]
            s.wait("gpsimd", "s_dv", dv_marks[dc])
            def fs(E, dc=dc):
                return E.dma_start(
                    out=self.weff_d.ap()[dc * 128:(dc + 1) * 128, :],
                    in_=self.weff_sb[:, :])
            s.op("gpsimd", fs, inc="s_ps", inc_val=16)
            ps_marks[dc] = s.c["s_ps"]
        self.presum_done = s.c["s_ps"]
        self.presum_dv = dv_marks[DCH - 1]

    def ph_collectives_attn(self, sub):
        s = self.s
        tag = "1" if sub == 1 else "2"
        s.wait("vector", "s_dv", self.nrm_last)
        for i in range(4):
            s.op("vector", lambda E, i=i: E.tensor_copy(
                self.kmx_sb[:, i:i + 1], self.km2[:, :]),
                inc="s_dv" if i == 3 else None)
        s.wait("sync", "s_dv")
        s.op("sync", lambda E: E.dma_start(out=self.km_in.ap(),
                                           in_=self.kmx_sb[:, :]),
             inc="s_st", inc_val=16)
        km_st = s.c["s_st"]
        s.wait("gpsimd", "s_st", km_st)
        s.op("gpsimd", lambda E: E.collective_compute(
            "AllReduce", ALU.max, replica_groups=[list(range(NCORE))],
            ins=[self.km_in.ap().opt()], outs=[self.km_out.ap().opt()]),
            inc="s_cc_m" + tag)
        s.op("gpsimd", lambda E: E.collective_compute(
            "AllGather", ALU.bypass, replica_groups=[list(range(NCORE))],
            ins=[self.kin_d.ap().opt()], outs=[self.kag_d.ap().opt()]),
            inc="s_cc_k" + tag)
        s.op("gpsimd", lambda E: E.collective_compute(
            "AllGather", ALU.bypass, replica_groups=[list(range(NCORE))],
            ins=[self.vin_d.ap().opt()], outs=[self.vag_d.ap().opt()]),
            inc="s_cc_v" + tag)

    def ph_attention(self, sub):
        s = self.s
        P = self.P
        a = (sub == 1)
        nqh = NQH_A if a else 1
        q_dram = self.qa_d if a else self.qb_d
        o_dram = self.ao_d if a else self.aob_d
        tag = "1" if a else "2"
        q_st = self.q_st
        lnx_free_pe = self.q_pe

        s.wait("sync", "s_cc_m" + tag, 1)
        s.op("sync", lambda E: E.dma_start(out=self.kmx_sb[:, :],
                                           in_=self.km_out.ap()),
             inc="s_ld", inc_val=16)
        s.wait("vector", "s_ld")
        s.op("vector", lambda E: E.tensor_mul(
            self.tiny[:, 1:2], self.qm2[:, :], self.kmx_sb[:, 0:1]), inc="s_dv")
        s.wait("scalar", "s_dv")
        s.op("scalar", lambda E: E.activation(
            self.tiny[:, 2:3], self.tiny[:, 1:2], AF.Ln), inc="s_ac")
        s.wait("scalar", "s_ac")
        s.op("scalar", lambda E: E.activation(
            self.tiny[:, 3:4], self.tiny[:, 2:3], AF.Exp, scale=0.5), inc="s_ac")
        s.wait("vector", "s_ac")
        s.op("vector", lambda E: E.tensor_scalar_mul(
            self.tiny[:, 4:5], self.tiny[:, 3:4], -SOFT), inc="s_dv")
        s.wait("tensor", "s_dv")
        s.wait("tensor", "s_ac")
        s.op("tensor", lambda E: E.matmul(
            P[6][:, 0:1], self.ones_f1[:, :], self.tiny[:, 4:5],
            start=True, stop=True), inc="s_pe")
        s.wait("vector", "s_pe")
        s.op("vector", lambda E: E.tensor_copy(self.negc_b[:, :], P[6][:, 0:1]))
        s.wait("vector", "s_pe", self.q_pe)
        for i in range(nqh):
            s.op("vector", lambda E, i=i: E.memset(
                self.sum_acc[:, i * TG:(i + 1) * TG], 0.0),
                inc="s_dv" if i == nqh - 1 else None)
        negc_dv = s.c["s_dv"]
        s.wait("scalar", "s_dv", negc_dv)
        fin_guard = negc_dv

        ld_q = {}
        ld_kv = {}
        pe_q = {}
        pe_unit = {}
        fin_dv = {}
        fin_st = {}
        add_marks = {}
        pv_marks = {}
        exp_marks = {}
        kag3 = self.kag_d.ap().rearrange("(r d) t -> r d t", d=KV)
        vag3 = self.vag_d.ap().rearrange("(r p) (h f) -> p r h f", p=128, h=8)
        nu = 2 * NKV
        N = NCH * nqh
        fi = 0
        for u in range(nu):
            h, qt = u // 2, u % 2
            if u == 0:
                s.wait("sync", "s_st", q_st)
            else:
                s.wait("sync", "s_pe", pe_unit[u - 1])
            for j in range(nqh):
                def fq(E, h=h, qt=qt, j=j, q_dram=q_dram, nqh=nqh):
                    r0 = (h * nqh + j) * 128
                    return E.dma_start(
                        out=self.qtl[:, j * TG:(j + 1) * TG],
                        in_=q_dram[r0:r0 + 128, qt * TG:(qt + 1) * TG])
                s.op("sync", fq, inc="s_ld", inc_val=16)
            ld_q[u] = s.c["s_ld"]
            def sched_kv(u, qtr):
                par = qtr % 2
                prev = (u, qtr - 2) if qtr >= 2 else (u - 1, qtr + 2)
                if prev in pe_q:
                    s.wait("sync", "s_pe", pe_q[prev])
                    s.wait("sync", "s_dv",
                           add_marks[prev[0], (prev[1] + 1) * QCH * nqh - 1])
                elif u == 0 and qtr < 2:
                    s.wait("sync", "s_pe", lnx_free_pe)
                    s.wait("sync", "s_cc_k" + tag, 1)
                    s.wait("sync", "s_cc_v" + tag, 1)
                r0 = qtr * 2
                hh = u // 2
                def fk(E, hh=hh, r0=r0, par=par):
                    src = kag3[r0:r0 + 2, hh * 128:(hh + 1) * 128, :].rearrange(
                        "r p t -> p r t")
                    dst = self.kh[:, par * 2048:(par + 1) * 2048].rearrange(
                        "p (r t) -> p r t", r=2)
                    return E.dma_start(out=dst, in_=src)
                s.op("sync", fk, inc="s_ld", inc_val=16)
                def fv(E, hh=hh, r0=r0, par=par):
                    src = vag3[:, r0:r0 + 2, hh, :]
                    dst = self.vh[:, par * 2048:(par + 1) * 2048].rearrange(
                        "p (r f) -> p r f", r=2)
                    return E.dma_start(out=dst, in_=src)
                s.op("sync", fv, inc="s_ld", inc_val=16)
                ld_kv[u, qtr] = s.c["s_ld"]

            sched_kv(u, 0)
            sched_kv(u, 1)
            exp_base = s.c["s_ac"]
            for n in range(N):
                kc, j = n // nqh, n % nqh
                qtr, par = kc // QCH, (kc // QCH) % 2
                koff = par * 2048 + (kc % QCH) * 128
                if j == 0 and kc % QCH == 0:
                    s.wait("tensor", "s_ld", ld_kv[u, qtr])
                    if kc == 0 and u == 0:
                        s.wait("tensor", "s_ld", ld_q[u])
                        s.wait("tensor", "s_ac")
                        s.wait("tensor", "s_dv")
                    elif kc == 0:
                        s.wait("tensor", "s_ld", ld_q[u])
                if n >= 1:
                    s.wait("tensor", "s_ac", exp_base + n)
                s.op("tensor", lambda E, koff=koff, j=j, n=n: E.matmul(
                    P[n % 2][:, :], self.kh[:, koff:koff + 128],
                    self.qtl[:, j * TG:(j + 1) * TG], start=True, stop=True),
                    inc="s_pe")
                s_mark = s.c["s_pe"]
                if n >= 1:
                    m = n - 1
                    mk, mj = m // nqh, m % nqh
                    mkoff = ((mk // QCH) % 2) * 2048 + (mk % QCH) * 128
                    if mk == 0 and (u - 1, mj) in fin_dv:
                        s.wait("tensor", "s_dv", fin_dv[u - 1, mj])
                    s.op("tensor", lambda E, mkoff=mkoff, mj=mj, m=m, mk=mk:
                         E.matmul(
                        P[2 + mj][:, :], self.vh[:, mkoff:mkoff + 128],
                        self.p_sb[:, (m % 4) * TG:(m % 4 + 1) * TG],
                        start=(mk == 0), stop=(mk == NCH - 1)), inc="s_pe")
                    pv_marks[u, m] = s.c["s_pe"]
                if j == nqh - 1 and kc % QCH == QCH - 1:
                    pe_q[u, qtr] = s.c["s_pe"]
                s.wait("scalar", "s_pe", s_mark)
                if n >= 4:
                    s.wait("scalar", "s_dv", add_marks[u, n - 4])
                elif u >= 1:
                    s.wait("scalar", "s_dv", add_marks[u - 1, N - 4 + n])
                s.op("scalar", lambda E, n=n: E.activation(
                    self.p_sb[:, (n % 4) * TG:(n % 4 + 1) * TG], P[n % 2][:, :],
                    AF.Exp, bias=self.negc_b[:, :], scale=SOFT), inc="s_ac")
                exp_marks[u, n] = s.c["s_ac"]
                if n >= 1:
                    m = n - 1
                    s.wait("vector", "s_pe", pv_marks[u, m])
                    s.op("vector", lambda E, m=m, nqh=nqh: E.tensor_tensor(
                        self.sum_acc[:, (m % nqh) * TG:(m % nqh + 1) * TG],
                        self.p_sb[:, (m % 4) * TG:(m % 4 + 1) * TG],
                        self.sum_acc[:, (m % nqh) * TG:(m % nqh + 1) * TG],
                        ALU.add), inc="s_dv")
                    add_marks[u, m] = s.c["s_dv"]
                    for q0 in (0, 1):
                        if m == (q0 + 1) * QCH * nqh - 1:
                            sched_kv(u, q0 + 2)
            pe_unit[u] = s.c["s_pe"]
            m = N - 1
            mk, mj = m // nqh, m % nqh
            mkoff = ((mk // QCH) % 2) * 2048 + (mk % QCH) * 128
            s.wait("tensor", "s_ac", exp_marks[u, m])
            s.op("tensor", lambda E, mkoff=mkoff, mj=mj, m=m, mk=mk: E.matmul(
                P[2 + mj][:, :], self.vh[:, mkoff:mkoff + 128],
                self.p_sb[:, (m % 4) * TG:(m % 4 + 1) * TG],
                start=(mk == 0), stop=(mk == NCH - 1)), inc="s_pe")
            pv_marks[u, m] = s.c["s_pe"]
            s.wait("vector", "s_pe", pv_marks[u, m])
            s.op("vector", lambda E, m=m, nqh=nqh: E.tensor_tensor(
                self.sum_acc[:, (m % nqh) * TG:(m % nqh + 1) * TG],
                self.p_sb[:, (m % 4) * TG:(m % 4 + 1) * TG],
                self.sum_acc[:, (m % nqh) * TG:(m % nqh + 1) * TG],
                ALU.add), inc="s_dv")
            add_marks[u, m] = s.c["s_dv"]
            for j in range(nqh):
                last_n = (NCH - 1) * nqh + j
                s.wait("tensor", "s_dv", add_marks[u, last_n])
                s.wait("tensor", "s_dv", fin_guard)
                s.op("tensor", lambda E, j=j: E.matmul(
                    P[6][0:1, :], self.ones_fc[:, :],
                    self.sum_acc[:, j * TG:(j + 1) * TG], start=True, stop=True),
                    inc="s_pe")
                summ = s.c["s_pe"]
                s.wait("vector", "s_pe", summ)
                s.op("vector", lambda E: E.reciprocal(
                    self.inv_vec[:, :], P[6][0:1, :]))
                s.op("vector", lambda E, j=j: E.memset(
                    self.sum_acc[:, j * TG:(j + 1) * TG], 0.0), inc="s_dv")
                s.wait("tensor", "s_dv")
                s.op("tensor", lambda E: E.matmul(
                    P[7][:, :], self.ones_f1[:, :], self.inv_vec[:, :],
                    start=True, stop=True), inc="s_pe")
                invm = s.c["s_pe"]
                s.wait("vector", "s_pe", invm)
                s.op("vector", lambda E: E.tensor_copy(
                    self.inv_sb[:, :], P[7][:, :]))
                if fi >= 2:
                    s.wait("vector", "s_st", fin_st[fi - 2])
                s.op("vector", lambda E, j=j, fi=fi: E.tensor_tensor(
                    self.stg_ao[:, (fi % 2) * TG:(fi % 2 + 1) * TG],
                    P[2 + j][:, :], self.inv_sb[:, :], ALU.mult), inc="s_dv")
                fin_dv[u, j] = s.c["s_dv"]
                fin_guard = fin_dv[u, j]
                s.wait("sync", "s_dv", fin_dv[u, j])
                def fso(E, h=h, j=j, qt=qt, fi=fi, o_dram=o_dram, nqh=nqh):
                    r0 = (h * nqh + j) * 128
                    return E.dma_start(
                        out=o_dram[r0:r0 + 128, qt * TG:(qt + 1) * TG],
                        in_=self.stg_ao[:, (fi % 2) * TG:(fi % 2 + 1) * TG])
                s.op("sync", fso, inc="s_st", inc_val=16)
                fin_st[fi] = s.c["s_st"]
                fi += 1
        self.attn_st = s.c["s_st"]

    def ph_wo(self, sub, tg):
        s = self.s
        P = self.P
        a = (sub == 1)
        wo = self.wo_a if a else self.wo_b
        o_dram = self.ao_d if a else self.aob_d
        ndc = DCH if a else 8
        s.wait("sync", "s_st", self.attn_st)
        if tg == 1:
            s.wait("sync", "s_pe", self.wo_pe)
        def fld(E, o_dram=o_dram, ndc=ndc, tg=tg):
            src = o_dram.ap().rearrange("(dc p) t -> p dc t", p=128)[
                :, 0:ndc, tg * TG:(tg + 1) * TG]
            dst = self.lnx[:, 0:ndc * TG].rearrange("p (dc t) -> p dc t", t=TG)
            return E.dma_start(out=dst, in_=src)
        s.op("sync", fld, inc="s_ld", inc_val=16)
        ao_ld = s.c["s_ld"]
        wbuf_guard = s.c["s_pe"]
        s.wait("tensor", "s_ac")
        s.wait("tensor", "s_dv")
        pe_marks = []
        dv_marks = []
        for oc in range(DCH):
            s.wait("gpsimd", "s_pe",
                   pe_marks[oc - 2] if oc >= 2 else wbuf_guard)
            if oc == 0 and self.presum_done:
                s.wait("gpsimd", "s_ps", self.presum_done)
            wm = self.weight_col_load(wo, oc, oc % 2, nrows=ndc * 128)
            s.wait("tensor", "s_w", wm)
            if oc == 0:
                s.wait("tensor", "s_ld", ao_ld)
            if oc >= 4:
                s.wait("tensor", "s_dv", dv_marks[oc - 4])
            bank = oc % 4
            for dc in range(ndc):
                s.op("tensor", lambda E, oc=oc, dc=dc, bank=bank: E.matmul(
                    P[bank][:, :], self.wslot(oc % 2, dc), self.lx(dc),
                    start=(dc == 0), stop=(dc == ndc - 1)),
                    inc="s_pe" if dc == ndc - 1 else None)
            pe_marks.append(s.c["s_pe"])
            s.wait("vector", "s_pe", pe_marks[oc])
            s.op("vector", lambda E, oc=oc, bank=bank, tg=tg: E.tensor_tensor(
                self.xs(oc, tg), P[bank][:, :], self.xs(oc, tg), ALU.add),
                inc="s_dv")
            dv_marks.append(s.c["s_dv"])
        self.resid_dv = s.c["s_dv"]
        self.wo_pe = pe_marks[-1]

    def ph_partials(self, tg):
        s = self.s
        P = self.P
        s.op("vector", lambda E, tg=tg: E.scalar_tensor_tensor(
            self.w1_v[:, :], self.rstd_v[:, :], RMEAN,
            self.mask_b[:, tg * TG:(tg + 1) * TG], op0=ALU.mult, op1=ALU.mult))
        s.op("vector", lambda E: E.tensor_mul(
            self.sm_b[:, :], self.mean_v[:, :], self.w1_v[:, :]))
        s.op("vector", lambda E, tg=tg: E.reduce_sum(
            self.tiny[:, 12 + tg:13 + tg], self.sm_b[:, :], AX.X), inc="s_dv")
        if tg == 1:
            s.wait("vector", "s_dv")
            s.op("vector", lambda E: E.tensor_add(
                self.tiny[:, 8:9], self.tiny[:, 12:13], self.tiny[:, 13:14]),
                inc="s_dv")
        s.wait("tensor", "s_dv")
        s.wait("tensor", "s_ac")
        s.op("tensor", lambda E: E.matmul(
            P[6][:, :], self.ones_f1[:, :], self.w1_v[:, :],
            start=True, stop=True), inc="s_pe")
        s.wait("vector", "s_pe")
        qp = self.qp0 if tg == 0 else self.qp1
        for dc in range(DCH):
            s.op("vector", lambda E, dc=dc, tg=tg, qp=qp: E.scalar_tensor_tensor(
                self.rmul[:, :], self.xs(dc, tg), 1.0, P[6][:, :],
                op0=ALU.mult, op1=ALU.mult, accum_out=qp[:, dc:dc + 1]))
        s.op("vector", lambda E: E.engine_nop(), inc="s_dv")

    def ph_router(self, sub):
        s = self.s
        P = self.P
        tag = "1" if sub == 2 else "2"
        s.op("vector", lambda E: E.tensor_add(
            self.qp0[:, :], self.qp0[:, :], self.qp1[:, :]), inc="s_dv")
        s.wait("tensor", "s_dv")
        s.op("tensor", lambda E: E.matmul(
            P[6][:, 0:1], self.ones_f1[:, :], self.tiny[:, 8:9],
            start=True, stop=True), inc="s_pe")
        s.wait("vector", "s_pe")
        s.op("vector", lambda E: E.tensor_copy(self.t2b[:, :], P[6][:, 0:1]),
             inc="s_dv")
        s.wait("vector", "s_dv")
        s.op("vector", lambda E: E.tensor_scalar(
            self.qp0[:, :], self.qp0[:, :], self.t2b[:, :], None,
            op0=ALU.subtract), inc="s_dv")
        s.wait("sync", "s_dv")
        s.op("sync", lambda E: E.dma_start(out=self.rtr_in.ap(),
                                           in_=self.qp0[:, :]),
             inc="s_st", inc_val=16)
        s.wait("gpsimd", "s_st")
        s.op("gpsimd", lambda E: E.collective_compute(
            "AllReduce", ALU.add, replica_groups=[list(range(NCORE))],
            ins=[self.rtr_in.ap().opt()], outs=[self.rtr_out.ap().opt()]),
            inc="s_cc_r" + tag)
        s.wait("sync", "s_cc_r" + tag, 1)
        s.op("sync", lambda E: E.dma_start(out=self.q_sb[:, :],
                                           in_=self.rtr_out.ap()),
             inc="s_ld", inc_val=16)
        s.wait("tensor", "s_ld")
        for dc in range(DCH):
            s.op("tensor", lambda E, dc=dc: E.matmul(
                P[7][0:1, 0:8], self.q_sb[:, dc:dc + 1],
                self.rw_sb[:, dc * 8:(dc + 1) * 8],
                start=(dc == 0), stop=(dc == DCH - 1)),
                inc="s_pe" if dc == DCH - 1 else None)
        s.wait("vector", "s_pe")
        s.op("vector", lambda E: E.tensor_tensor(
            self.lg[:, :], P[7][0:1, 0:8], self.rbv[:, :], ALU.add),
            inc="s_dv")
        s.wait("vector", "s_dv")
        s.op("vector", lambda E: E.reduce_max(
            self.tiny[:, 9:10], self.lg[:, :], AX.X, negate=True), inc="s_dv")
        s.wait("scalar", "s_dv")
        s.op("scalar", lambda E: E.activation(
            self.lg[:, :], self.lg[:, :], AF.Exp, bias=self.tiny[:, 9:10]),
            inc="s_ac")
        s.wait("vector", "s_ac")
        s.op("vector", lambda E: E.reduce_sum(
            self.tiny[:, 10:11], self.lg[:, :], AX.X), inc="s_dv")
        s.wait("vector", "s_dv")
        s.op("vector", lambda E: E.reciprocal(
            self.tiny[:, 11:12], self.tiny[:, 10:11]), inc="s_dv")
        s.wait("vector", "s_dv")
        s.op("vector", lambda E: E.tensor_scalar(
            self.lg[:, :], self.lg[:, :], self.tiny[:, 11:12], None,
            op0=ALU.mult), inc="s_dv")
        s.wait("sync", "s_dv")
        s.op("sync", lambda E: E.dma_start(
            out=self.rt8_d.ap().rearrange("e f -> f e"), in_=self.lg[:, :]),
            inc="s_st", inc_val=16)
        s.wait("sync", "s_st")
        s.op("sync", lambda E: E.dma_start(out=self.rtT[:, :],
                                           in_=self.rt8_d.ap()),
             inc="s_ld", inc_val=16)
        s.wait("tensor", "s_ld")
        s.wait("tensor", "s_dv")
        s.op("tensor", lambda E: E.matmul(
            P[6][:, 0:1], self.ind_sb[:, :], self.rtT[:, :],
            start=True, stop=True), inc="s_pe")
        s.wait("vector", "s_pe")
        s.op("vector", lambda E: E.tensor_copy(self.rbc[:, :], P[6][:, 0:1]),
             inc="s_dv")
        self.rbc_dv = s.c["s_dv"]

    def ph_ffn_tg(self, tg):
        s = self.s
        P = self.P
        wbuf_guard = getattr(self, "wbuf_free_pe", 0)
        s.wait("gpsimd", "s_pe", wbuf_guard)
        def fa(E):
            src = self.a_all.ap().rearrange("(dc p) e -> p dc e", p=128)
            dst = self.wbuf[:, 0:4096].rearrange("p (dc e) -> p dc e", e=ER)
            return E.dma_start(out=dst, in_=src)
        s.op("gpsimd", fa, inc="s_w", inc_val=16)
        aw = s.c["s_w"]
        s.wait("tensor", "s_w", aw)
        s.wait("tensor", "s_dv", self.lnx_dv)
        s.wait("tensor", "s_ac")
        for dc in range(DCH):
            s.op("tensor", lambda E, dc=dc: E.matmul(
                P[5][:, :], self.wslot(0, dc), self.lx(dc),
                start=(dc == 0), stop=(dc == DCH - 1)),
                inc="s_pe" if dc == DCH - 1 else None)
        tpe = s.c["s_pe"]
        s.wait("scalar", "s_pe", tpe)
        s.wait("scalar", "s_dv", self.rbc_dv)
        s.op("scalar", lambda E: E.activation(
            self.tprime[:, :], P[5][:, :], AF.Copy, scale=self.rbc[:, :]),
            inc="s_ac")
        tp_ac = s.c["s_ac"]
        pe_marks = []
        dv_marks = []
        for oc in range(DCH):
            if oc == 1:
                s.wait("gpsimd", "s_pe", tpe)
            elif oc >= 2:
                s.wait("gpsimd", "s_pe", pe_marks[oc - 2])
            wm = self.weight_col_load(self.wff, oc, (oc + 1) % 2)
            def fb(E, oc=oc):
                return E.dma_start(
                    out=self.ba_oc[:, (oc % 2) * 128:(oc % 2 + 1) * 128],
                    in_=self.b_all.ap()[:, oc * 128:(oc + 1) * 128])
            s.op("gpsimd", fb, inc="s_w", inc_val=16)
            bm = s.c["s_w"]
            s.wait("tensor", "s_w", bm)
            if oc == 0:
                s.wait("tensor", "s_ac", tp_ac)
                s.wait("tensor", "s_dv")
            if oc >= 4:
                s.wait("tensor", "s_dv", dv_marks[oc - 4])
            bank = oc % 4
            for dc in range(DCH):
                s.op("tensor", lambda E, oc=oc, dc=dc, bank=bank: E.matmul(
                    P[bank][:, :], self.wslot((oc + 1) % 2, dc), self.lx(dc),
                    start=(dc == 0), stop=False))
            s.op("tensor", lambda E, oc=oc, bank=bank: E.matmul(
                P[bank][:, :], self.ba_oc[:, (oc % 2) * 128:(oc % 2 + 1) * 128],
                self.tprime[:, :], start=False, stop=True), inc="s_pe")
            pe_marks.append(s.c["s_pe"])
            s.wait("vector", "s_pe", pe_marks[oc])
            s.op("vector", lambda E, oc=oc, bank=bank, tg=tg: E.scalar_tensor_tensor(
                self.xs(oc, tg), P[bank][:, :], self.bff_sb[:, oc:oc + 1],
                self.xs(oc, tg), op0=ALU.add, op1=ALU.add), inc="s_dv")
            dv_marks.append(s.c["s_dv"])
        self.resid_dv = s.c["s_dv"]
        self.wbuf_free_pe = pe_marks[-1]
        self.lnx_user_pe = pe_marks[-1]

    def sub_attn(self, sub):
        s = self.s
        a = (sub == 1)
        s.op("vector", lambda E: E.memset(self.qm2[:, :], 0.0))
        s.op("vector", lambda E: E.memset(self.km2[:, :], 0.0), inc="s_dv")
        for tg in range(NTG):
            self.ph_ln_stats(tg)
            self.ph_ln_apply(tg)
            if a:
                self.ph_proj(tg, self.wk_a, KV, self.kin_d, norm_acc="k")
                self.ph_proj_v(tg, self.wv_a)
            else:
                self.ph_proj(tg, self.wk_b, KV, self.kin_d, norm_acc="k")
                self.ph_proj_v(tg, self.wv_b)
            self.kv_st = self.proj_st
            if tg == NTG - 1:
                self.nrm_last = s.c["s_dv"]
            if a:
                self.ph_proj(tg, self.wq_a, D, self.qa_d, norm_acc="q")
            else:
                if tg == 0:
                    s.wait("gpsimd", "s_ps", self.presum_done)
                self.ph_proj(tg, self.weff_d, KV, self.qb_d, norm_acc="q")
            self.lnx_user_pe = self.proj_pe
        self.q_st = self.proj_st
        self.q_pe = self.proj_pe
        self.ph_collectives_attn(sub)
        if a:
            self.ph_presum_wqb()
        self.ph_attention(sub)
        for tg in range(NTG):
            self.ph_wo(sub, tg)
        self.lnx_user_pe = self.wo_pe
        s.barrier()

    def sub_ffn(self, sub):
        s = self.s
        for tg in range(NTG):
            self.ph_ln_stats(tg)
            self.ph_partials(tg)
            if tg == 0:
                s.op("vector", lambda E: E.tensor_copy(
                    self.sv_rstd[:, :], self.rstd_v[:, :]))
                s.op("vector", lambda E: E.tensor_copy(
                    self.sv_negmr[:, :], self.negmr_v[:, :]), inc="s_dv")
        self.ph_router(sub)
        for tg in range(NTG):
            if tg == 0:
                self.ph_ln_apply(tg, rstd=self.sv_rstd, negmr=self.sv_negmr)
            else:
                self.ph_ln_apply(tg)
            self.ph_ffn_tg(tg)
        del self.wbuf_free_pe
        s.barrier()

    def ph_store(self):
        s = self.s
        if self.dbg:
            for src, dst in ((self.qa_d, self.q_dbg), (self.kag_d, self.k_dbg),
                             (self.ao_d, self.ao_dbg), (self.vag_d, self.v_dbg)):
                s.op("sync", lambda E, src=src, dst=dst: E.dma_start(
                    out=dst.ap(), in_=src.ap()), inc="s_st", inc_val=16)
            if self.n_sub >= 2:
                s.wait("sync", "s_dv")
                s.op("sync", lambda E: E.dma_start(
                    out=self.r_dbg[:, 0:1], in_=self.rbc[:, :],
                    allow_slow_non_contiguous=True),
                    inc="s_st", inc_val=16)
                s.op("sync", lambda E: E.dma_start(
                    out=self.r_dbg[:, 1:33], in_=self.q_sb[:, :]),
                    inc="s_st", inc_val=16)
                s.op("sync", lambda E: E.dma_start(
                    out=self.r_dbg[0:1, 33:41], in_=self.lg[:, :]),
                    inc="s_st", inc_val=16)
                s.op("sync", lambda E: E.dma_start(
                    out=self.r_dbg[0:1, 41:49], in_=self.tiny[:, 8:16]),
                    inc="s_st", inc_val=16)
        s.wait("sync", "s_dv", self.resid_dv)
        s.op("sync", lambda E: E.dma_start(
            out=self.y_t.ap().rearrange("(dc p) t -> p dc t", p=128),
            in_=self.xsb[:, :].rearrange("p (dc t) -> p dc t", dc=DCH)),
            inc="s_st", inc_val=16)
        s.wait("sync", "s_st")

    def build(self):
        self.ph_init()
        seq = [(self.sub_attn, 1), (self.sub_ffn, 2),
               (self.sub_attn, 3), (self.sub_ffn, 4)]
        for fn, sub in seq[: self.n_sub]:
            fn(sub)
        self.ph_store()
        s = self.s
        with self.nc.Block() as block:
            block.tensor(lambda E: [f(E) for f in s.q["tensor"]])
            block.scalar(lambda E: [f(E) for f in s.q["scalar"]])
            block.vector(lambda E: [f(E) for f in s.q["vector"]])
            block.gpsimd(lambda E: [f(E) for f in s.q["gpsimd"]])
            block.sync(lambda E: [f(E) for f in s.q["sync"]])
        n = {e: len(s.q[e]) for e in ENGS}
        print(f"[kb] instructions: {n} total={sum(n.values())}")
        return self.nc


def host_inputs(inputs):
    f32 = np.float32
    ac = np.ascontiguousarray
    x = np.asarray(inputs["L0_attn_a_x"], f32)
    com = {
        "wq_a": ac(np.asarray(inputs["L0_attn_a_wq"], f32)),
        "wk_a": ac(np.asarray(inputs["L0_attn_a_wk"], f32)),
        "wv_a": ac(np.asarray(inputs["L0_attn_a_wv"], f32)),
        "wo_a": ac(np.asarray(inputs["L0_attn_a_wo"], f32)),
        "wq_b": ac(np.asarray(inputs["L0_attn_b_wq"], f32)),
        "wk_b": ac(np.asarray(inputs["L0_attn_b_wk"], f32)),
        "wv_b": ac(np.asarray(inputs["L0_attn_b_wv"], f32)),
        "wo_b": ac(np.asarray(inputs["L0_attn_b_wo"], f32)),
        "wff": ac(np.asarray(inputs["L0_ffn_W"], f32).T),
        "bff": ac(np.asarray(inputs["L0_ffn_b"], f32).reshape(DCH, 128).T),
        "a_all": ac(np.asarray(inputs["L0_ffn_A"], f32).transpose(2, 0, 1)
                    .reshape(D, ER)),
        "b_all": ac(np.asarray(inputs["L0_ffn_B"], f32).transpose(0, 2, 1)
                    .reshape(ER, D)),
        "rw_t": ac(np.asarray(inputs["L0_ffn_router_W"], f32).T),
        "rbv_i": ac(np.asarray(inputs["L0_ffn_router_b"], f32).reshape(1, 8)),
    }
    ind = np.zeros((8, ER), f32)
    for e in range(8):
        ind[e, e * 16:(e + 1) * 16] = 2.0
    com["ind_i"] = ind
    in_maps = []
    for c in range(NCORE):
        m = dict(com)
        m["x_t"] = ac(x[0, c * TC:(c + 1) * TC, :].T)
        t = np.arange(TC) + c * TC
        mrow = ((t >= 611) & (t <= S - 2)).astype(f32)
        m["mask_i"] = mrow.reshape(1, TC).astype(ml_dtypes.bfloat16)
        in_maps.append(m)
    return in_maps


_NC_CACHE = {}


def build_nc(n_sub=4, dbg=False):
    key = (n_sub, dbg)
    if key not in _NC_CACHE:
        _NC_CACHE[key] = KB(n_sub=n_sub, dbg=dbg).build()
    return _NC_CACHE[key]


def kernel(**inputs):
    nc = build_nc(4)
    in_maps = host_inputs(inputs)
    res = run_bass_kernel_spmd(nc, in_maps, core_ids=list(range(NCORE)))
    out = np.empty((1, S, D), np.float32)
    for c in range(NCORE):
        out[0, c * TC:(c + 1) * TC, :] = np.asarray(res.results[c]["y_t"]).T
    return out
